# version 01
# baseline (speedup 1.0000x reference)
import math
import numpy as np
import concourse.bass as bass
import concourse.mybir as mybir
from concourse.bass_utils import run_bass_kernel_spmd
from contextlib import ExitStack

F32 = mybir.dt.float32
BF16 = mybir.dt.bfloat16
I32 = mybir.dt.int32
AF = mybir.ActivationFunctionType
OP = mybir.AluOpType
AX = mybir.AxisListType

D = 1024
NC_ = 4
NDEP = 3
SCALING = 128.0 ** -0.5
TWO_PI = 2.0 * math.pi
INV2PI = 1.0 / TWO_PI
MAGIC = 12582912.0


class Prog:
    ENGS = ["T", "V", "S", "Q", "G"]

    def __init__(self):
        self.ops = {e: [] for e in self.ENGS}
        self.cnt = {e: 0 for e in self.ENGS}
        self.pend = {e: 0 for e in self.ENGS}
        self.waits = {e: {} for e in self.ENGS}

    def emit(self, e, fn):
        self.ops[e].append(("op", fn))
        self.pend[e] += 1

    def dma(self, e, fn):
        assert e in ("Q", "G")
        self.ops[e].append(("dma", fn))
        self.cnt[e] += 16

    def close(self, e):
        if e in ("Q", "G"):
            return self.cnt[e]
        if self.pend[e]:
            self.ops[e].append(("inc", None))
            self.cnt[e] += 1
            self.pend[e] = 0
        return self.cnt[e]

    def wait(self, e, src, val, allow_self=False):
        if src == e and not (allow_self and src in ("Q", "G")):
            return
        if val <= self.waits[e].get(src, 0):
            return
        self.waits[e][src] = val
        self.ops[e].append(("wait", (src, val)))


class Buf:
    def __init__(self, prog, name):
        self.p = prog
        self.name = name
        self.writer = None
        self.readers = {}

    def _resolve(self, eng, val):
        if val is None:
            return self.p.close(eng)
        return val

    def will_read(self, e):
        if self.writer and (self.writer[0] != e or e in ("Q", "G")):
            we, wv = self.writer
            wv = self._resolve(we, wv)
            self.writer = (we, wv)
            self.p.wait(e, we, wv, allow_self=True)

    def will_write(self, e):
        for re_ in list(self.readers):
            if re_ != e or e in ("Q", "G"):
                rv = self._resolve(re_, self.readers[re_])
                self.readers[re_] = rv
                self.p.wait(e, re_, rv, allow_self=True)
        if self.writer and (self.writer[0] != e or e in ("Q", "G")):
            we, wv = self.writer
            wv = self._resolve(we, wv)
            self.p.wait(e, we, wv, allow_self=True)

    def read(self, e):
        self.readers[e] = None

    def wrote(self, e):
        self.writer = (e, None)
        self.readers = {}


class Arena:
    def __init__(self, tensor, cols):
        self.t = tensor
        self.cols = cols
        self.off = 0
        self.marks = []

    def alloc(self, ncols_f32):
        ncols_f32 = (ncols_f32 + 1) & ~1
        assert self.off + ncols_f32 <= self.cols, f"arena overflow {self.off}+{ncols_f32}>{self.cols}"
        o = self.off
        self.off += ncols_f32
        return o

    def f32(self, ncols):
        o = self.alloc(ncols)
        return self.t.ap()[:, o:o + ncols]

    def bf16(self, ncols):
        o = self.alloc((ncols + 1) // 2)
        return self.t.ap().bitcast(BF16)[:, 2 * o:2 * o + ncols]

    def push(self):
        self.marks.append(self.off)

    def pop(self):
        self.off = self.marks.pop()


def build(S=4096, dbg=(), stop="all"):
    nc = bass.Bass("TRN2", target_bir_lowering=False)
    TILE = 512 if S >= 512 else S
    NTILE = S // TILE
    NG = TILE // 128
    NGT = S // 128
    dbg = set(dbg)

    din = lambda name, shape, dt=F32: nc.dram_tensor(name, shape, dt, kind="ExternalInput").ap()
    x_in = din("x", [S, D])
    wqkv_mov = din("wqkv_mov", [D, 3 * D])
    bqkv_row = din("bqkv_row", [1, 3 * D])
    wout_mov = din("wout_mov", [D, D])
    bout_row = din("bout_row", [1, D])
    cnw1 = din("cnw1", [D, D]); cnb1 = din("cnb1", [128, 8])
    cnw2 = din("cnw2", [D, 512]); cnb2 = din("cnb2", [128, 4])
    cnw3 = din("cnw3", [512, 256]); cnb3 = din("cnb3", [128, 2])
    cnw4 = din("cnw4", [256, 1]); cnb4 = din("cnb4", [1, 1])
    gcw1 = din("gcw1", [D, 512]); gcb1 = din("gcb1", [128, 4])
    gcw2 = din("gcw2", [512, 256]); gcb2 = din("gcb2", [128, 2])
    gcw3 = din("gcw3", [256, 1]); gcb3 = din("gcb3", [1, 1])
    ar_in = din("ar", [1, 1])
    phicol = din("phicol", [128, 1])
    th_row = din("th_row", [1, 21])
    ph_row = din("ph_row", [1, 21])
    qw00_row = din("qw00_row", [1, 4])
    logits_row = din("logits_row", [1, 4])
    ident_f = din("ident_f", [128, 128])
    ident_b = din("ident_b", [128, 128], BF16)
    maskT_f = din("maskT_f", [128, 128])
    ones_row_f = din("ones_row_f", [1, 128])
    ones_row_b = din("ones_row_b", [1, 128], BF16)
    ones2_row_b = din("ones2_row_b", [2, 128], BF16)

    out = nc.dram_tensor("out", [S, D], F32, kind="ExternalOutput").ap()
    x1_sp = nc.dram_tensor("x1_spill", [S, D], F32, kind="Internal").ap()
    mean_sp = nc.dram_tensor("mean_sp", [D], F32, kind="Internal").ap()
    rows_sp = nc.dram_tensor("rows_sp", [1, 128], F32, kind="Internal").ap()
    crow_sp = nc.dram_tensor("crow_sp", [512], F32, kind="Internal").ap()
    ln2_sp = nc.dram_tensor("ln2_spill", [S, D], F32, kind="Internal").ap()
    d_dbg = {}
    def dout(name, shape, dt=F32):
        if name not in d_dbg:
            d_dbg[name] = nc.dram_tensor("dbg_" + name, shape, dt, kind="ExternalOutput").ap()
        return d_dbg[name]

    es = ExitStack()
    P = Prog()
    bufs = {}
    def Bf(name):
        if name not in bufs:
            bufs[name] = Buf(P, name)
        return bufs[name]
    def rw(e, reads=(), writes=()):
        for b in reads:
            Bf(b).will_read(e)
        for b in writes:
            Bf(b).will_write(e)
        for b in reads:
            Bf(b).read(e)
        for b in writes:
            Bf(b).wrote(e)

    with es:
        sb = lambda n, s, dt=F32: es.enter_context(nc.sbuf_tensor(n, s, dt))
        psm = lambda n, s, dt=F32: es.enter_context(nc.psum_tensor(n, s, dt))

        cnw1_b = [sb(f"cnw1_b{k}", [128, D], BF16) for k in range(8)]
        cnw2_b = [sb(f"cnw2_b{k}", [128, 512], BF16) for k in range(8)]
        cnw3_b = [sb(f"cnw3_b{k}", [128, 256], BF16) for k in range(4)]
        cnw4_b = [sb(f"cnw4_b{k}", [128, 1], BF16) for k in range(2)]
        gcw1_b = [sb(f"gcw1_b{k}", [128, 512], BF16) for k in range(8)]
        gcw2_b = [sb(f"gcw2_b{k}", [128, 256], BF16) for k in range(4)]
        gcw3_b = [sb(f"gcw3_b{k}", [128, 1], BF16) for k in range(2)]
        bias_f = {}
        for nm, shape in [("cnb1", [128, 8]), ("cnb2", [128, 4]), ("cnb3", [128, 2]), ("cnb4", [1, 1]),
                          ("gcb1", [128, 4]), ("gcb2", [128, 2]), ("gcb3", [1, 1]), ("ar", [1, 1])]:
            bias_f[nm] = sb("sb_" + nm, shape)
        ident_fs = sb("ident_fs", [128, 128])
        ident_bs = sb("ident_bs", [128, 128], BF16)
        maskT_fs = sb("maskT_fs", [128, 128])
        ones_f = sb("ones_f", [1, 128])
        ones_b = sb("ones_b", [1, 128], BF16)
        ones2_b = sb("ones2_b", [2, 128], BF16)
        ones128b = sb("ones128b", [128, 128], BF16)
        cnw4_pad = sb("cnw4_pad", [128, 256], BF16)
        gcw3_pad = sb("gcw3_pad", [128, 256], BF16)
        phic = sb("phic", [128, 1])
        rowsA = sb("rowsA", [1, 128])
        rowsB = sb("rowsB", [1, 64])
        scal_sm = sb("scal_sm", [128, 256])
        qsml = sb("qsml", [128, 16])
        ABc = sb("ABc", [128, 64])
        c2cols = sb("c2cols", [128, 4])
        supw_c = sb("supw_c", [128, 12])
        eps12 = sb("eps12", [128, 1])
        maskS_b = [sb(f"maskS_b{c}", [128, 128], BF16) for c in range(NC_)]
        onesq = sb("onesq", [128, 128])
        dscr = sb("dscr", [128, 192])
        gfacc = sb("gfacc", [128, D])
        gfsm = sb("gfsm", [128, 10])
        gfsmb = sb("gfsmb", [128, 10], BF16)
        gfrow = sb("gfrow", [1, 16])
        stat = sb("stat", [128, 16])
        sqacc = sb("sqacc", [128, 36])
        bqkv_b = sb("bqkv_b", [2, 3 * D], BF16)
        bout_b = sb("bout_b", [2, D], BF16)

        AR_COLS = 38688
        arena_t = sb("arena", [128, AR_COLS])
        A = Arena(arena_t, AR_COLS)

        A.push()
        wqkv_b = [A.bf16(3 * D) for _ in range(8)]
        wout_b = [A.bf16(D) for _ in range(8)]
        A.push()
        wstage = A.f32(3 * D)
        A.pop()
        x_t = [A.f32(D) for _ in range(2)]
        qkv_tm = A.bf16(512)
        nv_tm = A.f32(D)
        xhat_tm = A.bf16(D)
        xhat_fm = A.bf16(D)
        st_q1 = A.f32(D); st_q2 = A.f32(D)
        st_k1 = A.f32(D); st_k2 = A.f32(D)
        ct_c = A.f32(256); ct_t = A.f32(256); ct_u = A.f32(256)
        stb = {n: A.bf16(D) for n in ["q1", "q2", "nq2", "k1", "k2"]}
        nv_fm = A.bf16(D)
        nvT_all = A.bf16(D)
        sraw = A.f32(D)
        e_c = A.f32(D)
        dnm = A.f32(128)
        rs2 = A.f32(128)
        w_b = A.bf16(D)
        wT_all = [A.bf16(D) for _ in range(NC_)]
        sqs = A.f32(D)
        sgn_s = A.f32(512)
        abs_s = sqs[:, 0:512]
        clp_s = sqs[:, 512:1024]
        acc_s = A.f32(128)
        acc_b = A.bf16(128)
        Rf = A.f32(128)
        Rmb = A.bf16(128)
        Ff = A.f32(128)
        meas_b = A.bf16(D)
        x1_t = A.f32(D)
        ln2_t = A.f32(D)
        a_peak = A.off
        A.pop()

        A.push()
        meanrow = A.f32(D)[0:1, :]
        gfaccb = A.bf16(D)
        A.pop()

        A.push()
        xc_t = [A.f32(D) for _ in range(2)]
        st_tm = A.f32(NG * D)
        st_tmb = A.bf16(NG * D)
        st_fmb = A.bf16(8 * TILE)
        h1_b = A.bf16(8 * TILE)
        h2_b = A.bf16(4 * TILE)
        h3_b = A.bf16(2 * TILE)
        tw1 = A.f32(512)
        tw2 = A.f32(512)
        outf_t = A.f32(D)
        tf_row = A.f32(TILE)[0:1, :]
        crow_f = A.f32(TILE)[0:1, :]
        ccol = A.f32(8)
        c_peak = A.off
        A.pop()
        assert max(a_peak, c_peak) <= AR_COLS

        bank_mm = psm("bank_mm", [128, 512])
        bank_mm2 = psm("bank_mm2", [128, 512])
        bank_s1 = psm("bank_s1", [128, 512])
        bank_s2 = psm("bank_s2", [128, 512])
        bank_wT = psm("bank_wT", [128, 512], BF16)
        bank_m = psm("bank_m", [128, 512])
        bank_tb = psm("bank_tb", [128, 1024], BF16)
        bank_tf = psm("bank_tf", [128, 512])

        SEM = {e: es.enter_context(nc.semaphore("sem_" + e)) for e in Prog.ENGS}

        def dma_load(q, dst_ap, src_ap, writes=(), reads=(), slow=False):
            for b in reads:
                Bf(b).will_read(q)
            for b in writes:
                Bf(b).will_write(q)
            def fn(eng, sem):
                eng.dma_start(out=dst_ap, in_=src_ap, allow_slow_non_contiguous=slow).then_inc(sem, 16)
            P.dma(q, fn)
            for b in reads:
                Bf(b).read(q)
            for b in writes:
                Bf(b).wrote(q)

        STAGES = ["pre", "proA", "pro", "ln1", "qkv", "trans", "ent", "pm", "w", "m", "wout", "phaseA", "gf", "seed", "cyc", "all"]
        def go(st):
            return STAGES.index(stop) >= STAGES.index(st)

        def barrier_pro():
            vals = {e: P.close(e) for e in Prog.ENGS}
            for e in Prog.ENGS:
                for s_, v in vals.items():
                    if s_ != e and v > 0:
                        P.wait(e, s_, v)
            for b in bufs.values():
                b.writer = None
                b.readers = {}

        dma_load("Q", ident_fs[:], ident_f, writes=("ident_fs",))
        dma_load("Q", ident_bs[:], ident_b, writes=("ident_bs",))
        dma_load("Q", maskT_fs[:], maskT_f, writes=("maskT_fs",))
        dma_load("Q", ones_f[:], ones_row_f, writes=("ones_f",))
        dma_load("Q", ones_b[:], ones_row_b, writes=("ones_b",))
        dma_load("Q", ones2_b[:], ones2_row_b, writes=("ones2_b",))
        dma_load("Q", phic[:], phicol, writes=("phic",))
        dma_load("Q", rowsA.ap()[:, 0:21], th_row, writes=("rowsA",))
        dma_load("Q", rowsA.ap()[:, 21:42], ph_row, writes=("rowsA",))
        dma_load("Q", rowsB.ap()[:, 0:4], qw00_row, writes=("rowsB",))
        dma_load("Q", rowsB.ap()[:, 4:8], logits_row, writes=("rowsB",))
        for nm, src in [("cnb1", cnb1), ("cnb2", cnb2), ("cnb3", cnb3), ("cnb4", cnb4),
                        ("gcb1", gcb1), ("gcb2", gcb2), ("gcb3", gcb3), ("ar", ar_in)]:
            dma_load("Q", bias_f[nm][:], src, writes=("b_" + nm,))

        rw("V", (), ("eps12", "onesq", "gfacc", "ones128b", "pads", "dscr"))
        def init_v(v):
            v.memset(eps12[:], 1e-12)
            v.memset(onesq[:], 1.0)
            v.memset(gfacc[:], 0.0)
            v.memset(ones128b[:], 1.0)
            v.memset(cnw4_pad[:], 0.0)
            v.memset(gcw3_pad[:], 0.0)
            v.memset(dscr[:], 0.0)
        P.emit("V", init_v)

        dma_load("Q", wstage[0:1, 0:3 * D], bqkv_row, writes=("wstage",), reads=("wstage",))
        rw("V", ("wstage",), ("bqkv_b",))
        def bq_v(v):
            v.memset(bqkv_b[:], 0.0)
            v.tensor_copy(bqkv_b.ap()[0:1, :], wstage[0:1, 0:3 * D])
        P.emit("V", bq_v)
        dma_load("Q", wstage[0:1, 0:D], bout_row, writes=("wstage",), reads=("wstage",))
        rw("V", ("wstage",), ("bout_b",))
        def bo_v(v):
            v.memset(bout_b[:], 0.0)
            v.tensor_copy(bout_b.ap()[0:1, :], wstage[0:1, 0:D])
        P.emit("V", bo_v)

        def wprep(src, chunks, width, tiles):
            for k in range(chunks):
                dma_load("Q", wstage[:, 0:width], src[k * 128:(k + 1) * 128, 0:width],
                         writes=("wstage",), reads=("wstage",))
                rw("V", ("wstage",), ("W",))
                t = tiles[k]
                if callable(getattr(t, "ap", None)):
                    t = t.ap()
                P.emit("V", lambda v, t=t, width=width: v.tensor_copy(t[:, 0:width], wstage[:, 0:width]))
        wprep(wqkv_mov, 8, 3 * D, wqkv_b)
        wprep(wout_mov, 8, D, wout_b)
        wprep(cnw1, 8, D, cnw1_b)
        wprep(cnw2, 8, 512, cnw2_b)
        wprep(cnw3, 4, 256, cnw3_b)
        wprep(cnw4, 2, 1, cnw4_b)
        wprep(gcw1, 8, 512, gcw1_b)
        wprep(gcw2, 4, 256, gcw2_b)
        wprep(gcw3, 2, 1, gcw3_b)
        rw("V", ("W",), ("pads",))
        def pad_v(v):
            for kk in range(2):
                v.tensor_copy(cnw4_pad.ap()[:, kk * 128:kk * 128 + 1], cnw4_b[kk][:, 0:1])
                v.tensor_copy(gcw3_pad.ap()[:, kk * 128:kk * 128 + 1], gcw3_b[kk][:, 0:1])
        P.emit("V", pad_v)

        def dmy(v):
            v.tensor_tensor(dscr.ap()[:, 96:192], dscr.ap()[:, 0:96], dscr.ap()[:, 0:96], OP.add)
            v.tensor_tensor(dscr.ap()[:, 0:96], dscr.ap()[:, 96:192], dscr.ap()[:, 96:192], OP.add)

        def sdmy(s):
            s.activation(dscr.ap()[:, 0:192], dscr.ap()[:, 0:192], AF.Copy)

        def fold_into(v, dst, src, n, scratch_off=0):
            t1 = scal_sm.ap()[:, scratch_off:scratch_off + n] if src.shape[0] == 128 else rowsA.ap()[:, 105:105 + n]
            dmy(v)
            v.tensor_scalar(t1, src, INV2PI, MAGIC, OP.mult, OP.add)
            dmy(v)
            v.tensor_scalar(t1, t1, MAGIC, None, OP.subtract)
            dmy(v)
            v.scalar_tensor_tensor(dst, t1, -TWO_PI, src, OP.mult, OP.add)
            dmy(v)

        if go("proA"):
            sc = scal_sm.ap()
            dma_load("G", sc[:, 0:21], th_row.partition_broadcast(128).squeeze(1), writes=("thb",))
            dma_load("G", sc[:, 21:42], ph_row.partition_broadcast(128).squeeze(1), writes=("phb",))
            dma_load("G", qsml.ap()[:, 0:4], qw00_row.partition_broadcast(128).squeeze(1), writes=("qlb",))
            dma_load("G", qsml.ap()[:, 4:8], logits_row.partition_broadcast(128).squeeze(1), writes=("qlb",))

            rw("V", ("thb",), ("v1",))
            def pv1(v):
                v.tensor_scalar(sc[:, 63:84], sc[:, 0:21], 0.5, None, OP.mult)
                dmy(v)
                fold_into(v, sc[:, 84:105], sc[:, 63:84], 21, 42)
                v.tensor_scalar(sc[:, 63:84], sc[:, 63:84], 1.0, math.pi / 2, OP.mult, OP.add)
                dmy(v)
                fold_into(v, sc[:, 189:210], sc[:, 63:84], 21, 42)
            P.emit("V", pv1)
            rw("S", ("v1",), ("stct",))
            def ps1(s):
                s.activation(sc[:, 105:126], sc[:, 84:105], AF.Sin)
                s.activation(sc[:, 126:147], sc[:, 189:210], AF.Sin)
                sdmy(s)
            P.emit("S", ps1)

            rw("V", ("phic",), ("c2t",))
            def pv2(v):
                v.tensor_scalar(c2cols.ap()[:, 3:4], phic[:], 2.0, None, OP.mult)
                dmy(v)
                v.tensor_scalar(sc[:, 210:211], c2cols.ap()[:, 3:4], 1.0, math.pi / 2, OP.mult, OP.add)
                dmy(v)
                fold_into(v, sc[:, 211:212], sc[:, 210:211], 1, 42)
            P.emit("V", pv2)
            rw("S", ("c2t",), ("c2s",))
            def ps2(s):
                s.activation(c2cols.ap()[:, 0:1], sc[:, 211:212], AF.Sin)
                sdmy(s)
            P.emit("S", ps2)
            rw("V", ("c2s",), ("c2cols",))
            def pv3(v):
                v.tensor_scalar(c2cols.ap()[:, 1:2], c2cols.ap()[:, 0:1], 2.0, None, OP.mult)
                v.tensor_scalar(c2cols.ap()[:, 2:3], c2cols.ap()[:, 0:1], -1.0, None, OP.mult)
                dmy(v)
            P.emit("V", pv3)

            rw("V", ("phb", "c2cols"), ("psit",))
            def pv4(v):
                psi = sc[:, 189:210]
                v.tensor_scalar(psi, sc[:, 21:42], c2cols.ap()[:, 3:4], None, OP.subtract)
                dmy(v)
                fold_into(v, sc[:, 63:84], psi, 21, 42)
                v.tensor_scalar(psi, psi, 1.0, math.pi / 2, OP.mult, OP.add)
                dmy(v)
                fold_into(v, sc[:, 84:105], psi, 21, 42)
            P.emit("V", pv4)
            rw("S", ("psit",), ("sincos",))
            def ps3(s):
                s.activation(sc[:, 147:168], sc[:, 63:84], AF.Sin)
                s.activation(sc[:, 168:189], sc[:, 84:105], AF.Sin)
                sdmy(s)
            P.emit("S", ps3)
            rw("V", ("sincos", "stct"), ("ABc",))
            def pv5(v):
                ctm1 = sc[:, 63:84]
                v.tensor_scalar(ctm1, sc[:, 126:147], 1.0, None, OP.subtract)
                stcos = sc[:, 84:105]
                v.tensor_tensor(stcos, sc[:, 105:126], sc[:, 168:189], OP.mult)
                dmy(v)
                v.tensor_tensor(ABc.ap()[:, 0:21], ctm1, stcos, OP.add)
                v.tensor_tensor(ABc.ap()[:, 42:63], ctm1, stcos, OP.subtract)
                v.tensor_tensor(ABc.ap()[:, 21:42], sc[:, 105:126], sc[:, 147:168], OP.mult)
                dmy(v)
            P.emit("V", pv5)

            rw("V", ("qlb",), ("sup1",))
            def pv6(v):
                q = qsml.ap()
                v.tensor_reduce(q[:, 8:9].unsqueeze(2), q[:, 4:8].unsqueeze(1), AX.X, OP.max)
                dmy(v)
                v.tensor_scalar(q[:, 9:13], q[:, 4:8], q[:, 8:9], None, OP.subtract)
                fold_into(v, q[:, 13:16], q[:, 0:3], 3, 42)
                fold_into(v, sc[:, 212:213], q[:, 3:4], 1, 42)
            P.emit("V", pv6)
            rw("S", ("sup1",), ("sup2",))
            def ps4(s):
                s.activation(qsml.ap()[:, 4:8], qsml.ap()[:, 9:13], AF.Exp)
                s.activation(qsml.ap()[:, 9:12], qsml.ap()[:, 13:16], AF.Sin)
                s.activation(qsml.ap()[:, 12:13], sc[:, 212:213], AF.Sin)
                sdmy(s)
            P.emit("S", ps4)
            rw("V", ("sup2",), ("supw_c",))
            def pv7(v):
                q = qsml.ap()
                v.tensor_reduce(q[:, 8:9].unsqueeze(2), q[:, 4:8].unsqueeze(1), AX.X, OP.add)
                dmy(v)
                v.reciprocal(q[:, 13:14], q[:, 8:9])
                dmy(v)
                v.tensor_scalar(supw_c.ap()[:, 0:4], q[:, 4:8], q[:, 13:14], None, OP.mult)
                v.tensor_scalar(q[:, 4:8], q[:, 9:13], 0.3, 0.5, OP.mult, OP.add)
                dmy(v)
                v.tensor_scalar(q[:, 9:13], q[:, 4:8], -1.0, 1.0, OP.mult, OP.add)
                dmy(v)
                v.tensor_tensor(q[:, 13:16], q[:, 4:7], q[:, 9:12], OP.mult)
                v.tensor_tensor(sc[:, 212:213], q[:, 7:8], q[:, 12:13], OP.mult)
                dmy(v)
                v.tensor_scalar(supw_c.ap()[:, 4:7], q[:, 13:16], SCALING, None, OP.mult)
                v.tensor_scalar(supw_c.ap()[:, 7:8], sc[:, 212:213], SCALING, None, OP.mult)
                dmy(v)
                v.tensor_scalar(supw_c.ap()[:, 8:12], supw_c.ap()[:, 0:4], 1e-6, None, OP.mult)
                dmy(v)
            P.emit("V", pv7)
            rw("V", ("supw_c", "maskT_fs"), ("maskS",))
            def pv8(v):
                for c in range(NC_):
                    v.tensor_scalar(maskS_b[c][:], maskT_fs[:], supw_c.ap()[:, c:c + 1], None, OP.mult)
            P.emit("V", pv8)
            if "pro" in dbg:
                dma_load("G", dout("scal", [128, 256]), scal_sm[:], reads=("sincos",))
                dma_load("G", dout("qsml", [128, 16]), qsml[:], reads=("sup2",))
                dma_load("G", dout("ABc", [128, 64]), ABc[:], reads=("ABc",))
                dma_load("G", dout("c2cols", [128, 4]), c2cols[:], reads=("c2cols",))
                dma_load("G", dout("supw", [128, 8]), supw_c.ap()[:, 0:8], reads=("supw_c",))

        barrier_pro()

        def g16view(ap_, h, width=16):
            return ap_.rearrange("p (g r) -> p g r", r=128)[:, :, h * width:(h + 1) * width]

        for it in (range(NTILE) if go("ln1") else ()):
            for g in range(NG):
                gi = it * NG + g
                tok0 = it * TILE + g * 128
                xbuf = x_t[gi % 2]
                xk = f"x_t{gi % 2}"
                dma_load("Q", xbuf, x_in[tok0:tok0 + 128, :], writes=(xk,))
                xg = xbuf

                rw("V", (xk,), ("stat",))
                P.emit("V", lambda v, xg=xg: v.tensor_reduce(stat.ap()[:, 0:1].unsqueeze(2), xg.unsqueeze(1), AX.X, OP.add))
                rw("S", (xk,), ("sqs", "stat"))
                def ln1_s(s, xg=xg):
                    s.activation(sqs[:, 0:512], xg[:, 0:512], AF.Square, accum_out=stat.ap()[:, 1:2])
                    s.activation(sqs[:, 0:512], xg[:, 512:1024], AF.Square, accum_out=stat.ap()[:, 2:3])
                P.emit("S", ln1_s)
                rw("V", ("stat",), ("stat",))
                def ln1_f(v):
                    st = stat.ap()
                    v.tensor_scalar(st[:, 3:4], st[:, 0:1], 1.0 / D, None, OP.mult)
                    v.tensor_tensor(st[:, 4:5], st[:, 1:2], st[:, 2:3], OP.add)
                    dmy(v)
                    v.tensor_scalar(st[:, 7:8], st[:, 4:5], 1.0 / D, None, OP.mult)
                    v.tensor_tensor(st[:, 5:6], st[:, 3:4], st[:, 3:4], OP.mult)
                    dmy(v)
                    v.tensor_tensor(st[:, 4:5], st[:, 7:8], st[:, 5:6], OP.subtract)
                    dmy(v)
                    v.tensor_scalar(st[:, 7:8], st[:, 4:5], 1e-5, None, OP.add)
                    dmy(v)
                    v.reciprocal(st[:, 5:6], st[:, 7:8])
                    dmy(v)
                P.emit("V", ln1_f)
                rw("S", ("stat",), ("stat",))
                def ln1_sq(s):
                    s.activation(stat.ap()[:, 6:7], stat.ap()[:, 5:6], AF.Sqrt)
                    sdmy(s)
                P.emit("S", ln1_sq)
                rw("V", (xk, "stat"), ("xhat_tm",))
                P.emit("V", lambda v, xg=xg: v.tensor_scalar(xhat_tm, xg, stat.ap()[:, 3:4], stat.ap()[:, 6:7], OP.subtract, OP.mult))
                if "xhat" in dbg and it == 0 and g == 0:
                    dma_load("G", dout("xhat", [128, D], BF16), xhat_tm, reads=("xhat_tm",))

                if not go("trans"):
                    continue
                rw("T", ("xhat_tm", "ident_bs", "xhat_fm"), ("bank_tb",))
                def xt_t(t):
                    for kk in range(8):
                        t.transpose(bank_tb.ap()[:, kk * 128:(kk + 1) * 128], xhat_tm[:, kk * 128:(kk + 1) * 128], ident_bs[:])
                P.emit("T", xt_t)
                rw("S", ("bank_tb",), ("xhat_fm",))
                P.emit("S", lambda s: s.copy(xhat_fm, bank_tb.ap()[:, 0:1024]))

                if not go("qkv"):
                    continue
                for ob in range(6):
                    bank = bank_mm if ob % 2 == 0 else bank_mm2
                    bkk = "bank_mm" if ob % 2 == 0 else "bank_mm2"
                    rw("T", ("xhat_fm", "ones_b", "bqkv_b", bkk + "r"), (bkk,))
                    def qkv_t(t, ob=ob, bank=bank):
                        for kk in range(8):
                            t.matmul(bank.ap()[:, 0:512], xhat_fm[:, kk * 128:(kk + 1) * 128],
                                     wqkv_b[kk][:, ob * 512:(ob + 1) * 512], start=(kk == 0), stop=False)
                        t.matmul(bank.ap()[:, 0:512], ones2_b[:], bqkv_b.ap()[:, ob * 512:(ob + 1) * 512],
                                 start=False, stop=True)
                    P.emit("T", qkv_t)
                    rw("S", (bkk,), ("sqs", "sqacc"))
                    def qkv_sq(s, ob=ob, bank=bank):
                        for hh in range(4):
                            s.activation(sqs[:, hh * 128:(hh + 1) * 128], bank.ap()[:, hh * 128:(hh + 1) * 128],
                                         AF.Square, accum_out=sqacc.ap()[:, ob * 4 + hh:ob * 4 + hh + 1])
                    P.emit("S", qkv_sq)
                    rw("V", ("sqacc",), ("sqacc2",))
                    def qkv_rs(v, ob=ob):
                        v.tensor_scalar(sqacc.ap()[:, 24:28], sqacc.ap()[:, ob * 4:ob * 4 + 4], 1e-24, None, OP.max)
                        dmy(v)
                        v.reciprocal(sqacc.ap()[:, 28:32], sqacc.ap()[:, 24:28])
                        dmy(v)
                    P.emit("V", qkv_rs)
                    rw("S", ("sqacc2",), ("sqacc3",))
                    def qkv_rs2(s):
                        s.activation(sqacc.ap()[:, 32:36], sqacc.ap()[:, 28:32], AF.Sqrt)
                        sdmy(s)
                    P.emit("S", qkv_rs2)
                    rw("S", (bkk, "sqacc3"), ("qkv_tm",))
                    def qkv_nm(s, bank=bank):
                        for hh in range(4):
                            s.activation(qkv_tm[:, hh * 128:(hh + 1) * 128], bank.ap()[:, hh * 128:(hh + 1) * 128],
                                         AF.Copy, scale=sqacc.ap()[:, 32 + hh:33 + hh])
                    P.emit("S", qkv_nm)
                    if ob >= 4:
                        rw("V", (bkk, "sqacc3"), ("nv_tm",))
                        def qkv_nmv(v, ob=ob, bank=bank):
                            for hh in range(4):
                                v.tensor_scalar(nv_tm[:, (ob - 4) * 512 + hh * 128:(ob - 4) * 512 + (hh + 1) * 128],
                                                bank.ap()[:, hh * 128:(hh + 1) * 128],
                                                sqacc.ap()[:, 32 + hh:33 + hh], None, OP.mult)
                        P.emit("V", qkv_nmv)
                    Bf(bkk + "r").wrote("S")
                    rw("T", ("qkv_tm", "ident_bs", "st_evac"), ("bank_tb",))
                    def tr_t(t):
                        for u in range(4):
                            t.transpose(bank_tb.ap()[:, u * 128:(u + 1) * 128], qkv_tm[:, u * 128:(u + 1) * 128], ident_bs[:])
                    P.emit("T", tr_t)
                    rw("V", ("bank_tb",), ("st_q1", "st_k1", "nv_fm", "st_evac"))
                    def tr_ev(v, ob=ob):
                        for u in range(4):
                            hh = ob * 4 + u
                            src = bank_tb.ap()[:, u * 128:(u + 1) * 128].rearrange("p (g r) -> p g r", r=16)
                            if hh < 8:
                                dst = g16view(st_q1, hh)
                            elif hh < 16:
                                dst = g16view(st_k1, hh - 8)
                            else:
                                dst = g16view(nv_fm, hh - 16)
                            v.tensor_copy(dst, src)
                    P.emit("V", tr_ev)
                if "nqkv" in dbg and it == 0 and g == 0:
                    dma_load("G", dout("stq1_init", [128, D]), st_q1, reads=("st_q1",))
                    dma_load("G", dout("nvfm", [128, D], BF16), nv_fm, reads=("nv_fm",))

                rw("V", ("st_q2w",), ("st_q2", "st_k2"))
                def zim(v):
                    v.memset(st_q2, 0.0)
                    v.memset(st_k2, 0.0)
                P.emit("V", zim)

                if not go("ent"):
                    continue
                for dep in range(NDEP):
                    for p in range(7):
                        dpi = dep * 7 + p
                        A1 = ABc.ap()[:, dpi:dpi + 1]
                        B1 = ABc.ap()[:, 21 + dpi:22 + dpi]
                        A2 = ABc.ap()[:, 42 + dpi:43 + dpi]
                        z_real = (dep == 0) or (p == 6)
                        c_real = (dep == 0)
                        for w1a, w2a, w1k, w2k in [(st_q1, st_q2, "st_q1", "st_q2"),
                                                    (st_k1, st_k2, "st_k1", "st_k2")]:
                            hp = g16view(w1a, p); hp_i = g16view(w2a, p)
                            hz = g16view(w1a, p + 1); hz_i = g16view(w2a, p + 1)
                            ctrl = ct_c.rearrange("p (g r) -> p g r", r=32)[:, :, 0:16]
                            sc_a = ct_t.rearrange("p (g r) -> p g r", r=32)[:, :, 0:16]
                            sc_b = ct_t.rearrange("p (g r) -> p g r", r=32)[:, :, 16:32]
                            sc_c = ct_u.rearrange("p (g r) -> p g r", r=32)[:, :, 0:16]
                            sc_d = ct_u.rearrange("p (g r) -> p g r", r=32)[:, :, 16:32]
                            rw("S", (w1k, w2k), ("ct_sq",))
                            def e_s1(s, hp=hp, hp_i=hp_i, sc_b=sc_b, sc_d=sc_d, c_real=c_real):
                                s.activation(sc_b, hp, AF.Square)
                                if not c_real:
                                    s.activation(sc_d, hp_i, AF.Square)
                            P.emit("S", e_s1)
                            rw("V", ("ct_sq",), ("ct_c",))
                            def e_v1(v, ctrl=ctrl, sc_b=sc_b, sc_d=sc_d, c_real=c_real):
                                if c_real:
                                    v.tensor_copy(ctrl, sc_b)
                                else:
                                    v.tensor_tensor(ctrl, sc_b, sc_d, OP.add)
                            P.emit("V", e_v1)
                            rw("S", (w1k, w2k), ("ct_t", "ct_u"))
                            def e_s2(s, hz=hz, hz_i=hz_i, sc_a=sc_a, sc_b=sc_b, sc_c=sc_c, sc_d=sc_d,
                                     A1=A1, B1=B1, A2=A2, z_real=z_real):
                                if z_real:
                                    s.activation(sc_a, hz, AF.Copy, scale=A1)
                                    s.activation(sc_c, hz, AF.Copy, scale=B1)
                                else:
                                    s.activation(sc_b, hz_i, AF.Copy, scale=B1)
                                    s.activation(sc_d, hz_i, AF.Copy, scale=A2)
                            P.emit("S", e_s2)
                            if not z_real:
                                rw("V", (w1k, "ct_t", "ct_u"), ("ct_t", "ct_u"))
                                def e_v2(v, hz=hz, sc_a=sc_a, sc_b=sc_b, sc_c=sc_c, sc_d=sc_d, A1=A1, B1=B1):
                                    v.scalar_tensor_tensor(sc_a, hz, A1, sc_b, OP.mult, OP.add)
                                    v.scalar_tensor_tensor(sc_c, hz, B1, sc_d, OP.mult, OP.add)
                                P.emit("V", e_v2)
                            rw("V", ("ct_c", "ct_t", "ct_u", w1k, w2k), (w1k, w2k))
                            def e_v3(v, hp=hp, hp_i=hp_i, hz=hz, hz_i=hz_i, ctrl=ctrl,
                                     sc_a=sc_a, sc_b=sc_b, sc_c=sc_c, sc_d=sc_d, z_real=z_real):
                                v.tensor_tensor(sc_b, ctrl, sc_a, OP.mult)
                                v.tensor_tensor(hp, hz, sc_b, OP.add)
                                v.tensor_tensor(sc_d, ctrl, sc_c, OP.mult)
                                if z_real:
                                    v.tensor_copy(hp_i, sc_d)
                                else:
                                    v.tensor_tensor(hp_i, hz_i, sc_d, OP.add)
                            P.emit("V", e_v3)
                rw("V", ("st_q1", "st_q2", "st_k1", "st_k2"), ("stb", "st_q2w"))
                def st_cast(v):
                    v.tensor_copy(stb["q1"], st_q1)
                    v.tensor_copy(stb["q2"], st_q2)
                    v.tensor_scalar(stb["nq2"], st_q2, -1.0, None, OP.mult)
                    v.tensor_copy(stb["k1"], st_k1)
                    v.tensor_copy(stb["k2"], st_k2)
                P.emit("V", st_cast)
                if "ent" in dbg and it == 0 and g == 0:
                    dma_load("G", dout("stq1", [128, D]), st_q1, reads=("st_q1",))
                    dma_load("G", dout("stq2", [128, D]), st_q2, reads=("st_q2",))

                if not go("pm"):
                    continue
                rw("V", ("nv_tm",), ("Rf",))
                def rf_v(v):
                    srcv = nv_tm.rearrange("p (j r) -> p r j", j=8)
                    v.tensor_reduce(Rf.rearrange("p r -> p r", ).unsqueeze(2), srcv, AX.X, OP.add)
                P.emit("V", rf_v)
                rw("V", ("Rf",), ("Rmb",))
                P.emit("V", lambda v: v.tensor_scalar(Rmb, Rf, 0.0, None, OP.is_ge))
                rw("T", ("Rmb", "ident_bs", "Ff_ev"), ("bank_wT",))
                P.emit("T", lambda t: t.transpose(bank_wT.ap()[:, 128:256], Rmb, ident_bs[:]))
                rw("V", ("bank_wT", "c2cols"), ("Ff", "Ff_ev"))
                def ff_v(v):
                    v.tensor_scalar(Ff, bank_wT.ap()[:, 128:256], c2cols.ap()[:, 1:2], c2cols.ap()[:, 2:3], OP.mult, OP.add)
                P.emit("V", ff_v)

                for half in range(2):
                    rw("T", ("nv_fm", "ident_bs", "nvT_ev"), ("bank_tb",))
                    def nv_t(t, half=half):
                        for u in range(4):
                            g16 = half * 4 + u
                            t.transpose(bank_tb.ap()[:, u * 128:(u + 1) * 128],
                                        nv_fm[:, g16 * 128:(g16 + 1) * 128], ident_bs[:])
                    P.emit("T", nv_t)
                    rw("S", ("bank_tb",), ("nvT_all", "nvT_ev"))
                    def nv_ev(s, half=half):
                        s.copy(nvT_all[:, half * 512:(half + 1) * 512], bank_tb.ap()[:, 0:512])
                    P.emit("S", nv_ev)

                for g16 in range(8):
                    sl = slice(g16 * 128, (g16 + 1) * 128)
                    rw("T", ("stb", "ps_done"), ("bank_s1", "bank_s2"))
                    def p_t(t, sl=sl):
                        t.matmul(bank_s1.ap()[:, 0:128], stb["q1"][:, sl], stb["k1"][:, sl], start=True, stop=False)
                        t.matmul(bank_s1.ap()[:, 0:128], stb["q2"][:, sl], stb["k2"][:, sl], start=False, stop=True)
                        t.matmul(bank_s2.ap()[:, 0:128], stb["q1"][:, sl], stb["k2"][:, sl], start=True, stop=False)
                        t.matmul(bank_s2.ap()[:, 0:128], stb["nq2"][:, sl], stb["k1"][:, sl], start=False, stop=True)
                    P.emit("T", p_t)
                    rw("S", ("bank_s1", "bank_s2"), ("sqs", "ps_done"))
                    def sr_s(s):
                        s.activation(sqs[:, 0:128], bank_s1.ap()[:, 0:128], AF.Square)
                        s.activation(sqs[:, 128:256], bank_s2.ap()[:, 0:128], AF.Square)
                    P.emit("S", sr_s)
                    rw("V", ("sqs",), ("sraw",))
                    def sr_v(v, sl=sl):
                        v.tensor_tensor(sraw[:, sl], sqs[:, 0:128], sqs[:, 128:256], OP.add)
                    P.emit("V", sr_v)
                if "sraw" in dbg and it == 0 and g == 0:
                    dma_load("G", dout("sraw", [128, D]), sraw, reads=("sraw",))

                if not go("w"):
                    continue
                for c in range(NC_):
                    al = supw_c.ap()[:, 4 + c:5 + c]
                    rw("S", ("sraw", "supw_c"), ("e_c",))
                    P.emit("S", lambda s, al=al: s.activation(e_c, sraw, AF.Exp, scale=al))
                    rw("V", ("e_c",), ("dnm",))
                    def dn_v(v):
                        src = e_c.rearrange("p (g j r) -> p g r j", j=8, r=16)
                        v.tensor_reduce(dnm.rearrange("p (g r) -> p g r", r=16).unsqueeze(3), src, AX.X, OP.add)
                    P.emit("V", dn_v)
                    rw("S", ("dnm",), ("rs2",))
                    P.emit("S", lambda s: s.activation(rs2, dnm, AF.Ln))
                    rw("S", ("rs2",), ("rs2",))
                    P.emit("S", lambda s: s.activation(rs2, rs2, AF.Exp, scale=-1.0))
                    rw("V", ("e_c", "rs2"), ("sqs",))
                    def p_v(v):
                        rsrep = rs2.rearrange("p (g r) -> p g r", r=16).unsqueeze(2).broadcast_to([128, 8, 8, 16])
                        v.tensor_tensor(sqs.rearrange("p (g j r) -> p g j r", j=8, r=16),
                                        e_c.rearrange("p (g j r) -> p g j r", j=8, r=16), rsrep, OP.mult)
                    P.emit("V", p_v)
                    rw("S", ("sqs", "eps12"), ("w_b",))
                    P.emit("S", lambda s: s.activation(w_b, sqs, AF.Sqrt, bias=eps12.ap()[:, 0:1]))
                    for g16 in range(8):
                        sl = slice(g16 * 128, (g16 + 1) * 128)
                        rw("T", ("w_b", "ident_bs", "wT_ev"), ("bank_wT",))
                        P.emit("T", lambda t, sl=sl: t.transpose(bank_wT.ap()[:, 0:128], w_b[:, sl], ident_bs[:]))
                        rw("V", ("bank_wT", "maskS"), ("wT_all", "wT_ev"))
                        def wt_ev(v, c=c, sl=sl):
                            v.tensor_tensor(wT_all[c][:, sl], bank_wT.ap()[:, 0:128], maskS_b[c][:], OP.mult)
                        P.emit("V", wt_ev)

                if not go("m"):
                    continue
                for g16 in range(8):
                    sl = slice(g16 * 128, (g16 + 1) * 128)
                    rw("T", ("wT_all", "nvT_all", "bank_m_r"), ("bank_m",))
                    def m_t(t, sl=sl):
                        for c in range(NC_):
                            t.matmul(bank_m.ap()[:, c * 128:(c + 1) * 128], wT_all[c][:, sl],
                                     nvT_all[:, sl], start=True, stop=True)
                    P.emit("T", m_t)
                    rw("S", ("bank_m",), ("sgn_s",))
                    P.emit("S", lambda s: s.activation(sgn_s, bank_m.ap()[:, 0:512], AF.Sign))
                    rw("V", ("bank_m", "sgn_s", "supw_c"), ("sqs", "acc_s", "bank_m_r"))
                    def clip_v(v):
                        v.tensor_scalar(abs_s.bitcast(I32), bank_m.ap().bitcast(I32)[:, 0:512], 0x7fffffff, None, OP.bitwise_and)
                        for c in range(NC_):
                            v.tensor_scalar(clp_s[:, c * 128:(c + 1) * 128], abs_s[:, c * 128:(c + 1) * 128],
                                            supw_c.ap()[:, 8 + c:9 + c], supw_c.ap()[:, c:c + 1], OP.max, OP.min)
                        v.tensor_tensor(abs_s, clp_s, sgn_s, OP.mult)
                        v.tensor_reduce(acc_s.unsqueeze(2), abs_s.rearrange("p (c d) -> p d c", c=4), AX.X, OP.add)
                    P.emit("V", clip_v)
                    rw("V", ("acc_s",), ("acc_b",))
                    P.emit("V", lambda v: v.tensor_copy(acc_b, acc_s))
                    rw("T", ("acc_b", "ident_bs", "meas_ev"), ("bank_tb",))
                    P.emit("T", lambda t: t.transpose(bank_tb.ap()[:, 0:128], acc_b, ident_bs[:]))
                    rw("V", ("bank_tb", "Ff"), ("meas_b", "meas_ev"))
                    def ms_ev(v, g16=g16):
                        dst = meas_b.rearrange("p (i r) -> p i r", r=128)[:, :, g16 * 16:(g16 + 1) * 16]
                        src = bank_tb.ap()[:, 0:128].rearrange("p (i r) -> p i r", r=16)
                        frep = Ff[:, g16 * 16:(g16 + 1) * 16].unsqueeze(1).broadcast_to([128, 8, 16])
                        v.tensor_tensor(dst, src, frep, OP.mult)
                    P.emit("V", ms_ev)
                if "meas" in dbg and it == 0 and g == 0:
                    dma_load("G", dout("meas", [128, D], BF16), meas_b, reads=("meas_b",))

                if not go("wout"):
                    continue
                for ob in range(2):
                    bank = bank_mm if ob == 0 else bank_mm2
                    bkk = "bank_mm" if ob == 0 else "bank_mm2"
                    rw("T", ("meas_b", "ones_b", "bout_b", bkk + "r"), (bkk,))
                    def wo_t(t, ob=ob, bank=bank):
                        for kk in range(8):
                            t.matmul(bank.ap()[:, 0:512], meas_b[:, kk * 128:(kk + 1) * 128],
                                     wout_b[kk][:, ob * 512:(ob + 1) * 512], start=(kk == 0), stop=False)
                        t.matmul(bank.ap()[:, 0:512], ones2_b[:], bout_b.ap()[:, ob * 512:(ob + 1) * 512],
                                 start=False, stop=True)
                    P.emit("T", wo_t)
                    rw("V", (bkk, xk), ("x1_t", bkk + "r"))
                    def x1_ev(v, ob=ob, bank=bank, xg=xg):
                        v.tensor_tensor(x1_t[:, ob * 512:(ob + 1) * 512], bank.ap()[:, 0:512],
                                        xg[:, ob * 512:(ob + 1) * 512], OP.add)
                    P.emit("V", x1_ev)
                dma_load("G", x1_sp[tok0:tok0 + 128, :], x1_t, reads=("x1_t",), writes=("x1_sp",))
                rw("V", ("x1_t",), ("stat",))
                P.emit("V", lambda v: v.tensor_reduce(stat.ap()[:, 8:9].unsqueeze(2), x1_t.unsqueeze(1), AX.X, OP.add))
                rw("S", ("x1_t",), ("sqs", "stat"))
                def ln2_s(s):
                    s.activation(sqs[:, 0:512], x1_t[:, 0:512], AF.Square, accum_out=stat.ap()[:, 9:10])
                    s.activation(sqs[:, 0:512], x1_t[:, 512:1024], AF.Square, accum_out=stat.ap()[:, 10:11])
                P.emit("S", ln2_s)
                rw("V", ("stat",), ("stat",))
                def ln2_f(v):
                    st = stat.ap()
                    v.tensor_scalar(st[:, 11:12], st[:, 8:9], 1.0 / D, None, OP.mult)
                    v.tensor_tensor(st[:, 12:13], st[:, 9:10], st[:, 10:11], OP.add)
                    dmy(v)
                    v.tensor_scalar(st[:, 15:16], st[:, 12:13], 1.0 / D, None, OP.mult)
                    v.tensor_tensor(st[:, 13:14], st[:, 11:12], st[:, 11:12], OP.mult)
                    dmy(v)
                    v.tensor_tensor(st[:, 12:13], st[:, 15:16], st[:, 13:14], OP.subtract)
                    dmy(v)
                    v.tensor_scalar(st[:, 15:16], st[:, 12:13], 1e-5, None, OP.add)
                    dmy(v)
                    v.reciprocal(st[:, 13:14], st[:, 15:16])
                    dmy(v)
                P.emit("V", ln2_f)
                rw("S", ("stat",), ("stat",))
                def ln2_sq(s):
                    s.activation(stat.ap()[:, 14:15], stat.ap()[:, 13:14], AF.Sqrt)
                    sdmy(s)
                P.emit("S", ln2_sq)
                rw("V", ("x1_t", "stat", "gfacc"), ("ln2_t", "gfacc"))
                def ln2_nm(v):
                    v.tensor_scalar(ln2_t, x1_t, stat.ap()[:, 11:12], stat.ap()[:, 14:15], OP.subtract, OP.mult)
                    v.tensor_tensor(gfacc[:], gfacc[:], ln2_t, OP.add)
                P.emit("V", ln2_nm)
                dma_load("G", ln2_sp[tok0:tok0 + 128, :], ln2_t, reads=("ln2_t",), writes=("ln2_sp",))

        def barrier():
            vals = {e: P.close(e) for e in Prog.ENGS}
            for e in Prog.ENGS:
                for s_, v in vals.items():
                    if s_ != e and v > 0:
                        P.wait(e, s_, v)
            for b in bufs.values():
                b.writer = None
                b.readers = {}

        barrier()
        if go("gf"):
            rw("V", ("gfacc",), ("gfaccb",))
            P.emit("V", lambda v: v.tensor_copy(gfaccb, gfacc[:]))
            for half in range(2):
                rw("T", ("gfaccb", "ones128b", "mr_ev"), ("bank_tf",))
                def gf_t1(t, half=half):
                    t.matmul(bank_tf.ap()[:, 0:512], ones128b[:], gfaccb[:, half * 512:(half + 1) * 512],
                             start=True, stop=True)
                P.emit("T", gf_t1)
                rw("V", ("bank_tf",), ("meanrow", "mr_ev"))
                def mr_ev(v, half=half):
                    v.tensor_scalar(meanrow[0:1, half * 512:(half + 1) * 512],
                                    bank_tf.ap()[0:1, 0:512], 1.0 / S, None, OP.mult)
                P.emit("V", mr_ev)
            dma_load("G", mean_sp.rearrange("(q d) -> q d", q=1), meanrow[0:1, :], reads=("meanrow",), writes=("mean_sp",))
            for f_ in range(8):
                dma_load("G", gfsm.ap()[:, f_:f_ + 1], mean_sp[f_ * 128:(f_ + 1) * 128].rearrange("(p q) -> p q", q=1),
                         reads=("mean_sp",), writes=("gfsm",))
            rw("V", ("gfsm",), ("gfsmb",))
            P.emit("V", lambda v: v.tensor_copy(gfsmb.ap()[:, 0:8], gfsm.ap()[:, 0:8]))
            rw("T", ("gfsmb",), ("bank_m",))
            def gf_t3(t):
                for o in range(4):
                    for kk in range(8):
                        t.matmul(bank_m.ap()[:, o:o + 1], gcw1_b[kk][:, o * 128:(o + 1) * 128],
                                 gfsmb.ap()[:, kk:kk + 1], start=(kk == 0), stop=(kk == 7))
            P.emit("T", gf_t3)
            rw("S", ("bank_m", "b_gcb1"), ("gfh1",))
            def gf_s1(s):
                for o in range(4):
                    s.activation(gfsmb.ap()[:, 2 + o:3 + o], bank_m.ap()[:, o:o + 1], AF.Gelu,
                                 bias=bias_f["gcb1"].ap()[:, o:o + 1])
                sdmy(s)
            P.emit("S", gf_s1)
            rw("T", ("gfh1",), ("bank_m2",))
            def gf_t4(t):
                for o in range(2):
                    for kk in range(4):
                        t.matmul(bank_m.ap()[:, 8 + o:9 + o], gcw2_b[kk][:, o * 128:(o + 1) * 128],
                                 gfsmb.ap()[:, 2 + kk:3 + kk], start=(kk == 0), stop=(kk == 3))
            P.emit("T", gf_t4)
            rw("S", ("bank_m2", "b_gcb2"), ("gfh2",))
            def gf_s2(s):
                for o in range(2):
                    s.activation(gfsmb.ap()[:, 6 + o:7 + o], bank_m.ap()[:, 8 + o:9 + o], AF.Gelu,
                                 bias=bias_f["gcb2"].ap()[:, o:o + 1])
                sdmy(s)
            P.emit("S", gf_s2)
            rw("T", ("gfh2", "pads"), ("bank_m3",))
            def gf_t5(t):
                for kk in range(2):
                    t.matmul(bank_m.ap()[:, 12:13], gcw3_pad.ap()[:, kk * 128:(kk + 1) * 128],
                             gfsmb.ap()[:, 6 + kk:7 + kk], start=(kk == 0), stop=(kk == 1))
            P.emit("T", gf_t5)
            rw("S", ("bank_m3", "b_gcb3"), ("gfrow",))
            def gf_sig(s):
                s.activation(gfrow.ap()[:, 0:1], bank_m.ap()[0:1, 12:13], AF.Sigmoid,
                             bias=bias_f["gcb3"].ap()[:, 0:1])
                sdmy(s)
            P.emit("S", gf_sig)
            rw("V", ("gfrow", "b_ar"), ("gfrow",))
            def gf_coef(v):
                v.tensor_scalar(gfrow.ap()[:, 1:2], bias_f["ar"].ap()[:, 0:1], 0.6, None, OP.mult)
                v.tensor_scalar(gfrow.ap()[:, 3:4], gfrow.ap()[:, 0:1], 0.7, -0.5, OP.mult, OP.add)
                dmy(v)
                v.tensor_tensor(gfrow.ap()[:, 4:5], gfrow.ap()[:, 3:4], bias_f["ar"].ap()[:, 0:1], OP.mult)
                dmy(v)
                v.tensor_scalar(gfrow.ap()[:, 2:3], gfrow.ap()[:, 4:5], 2.0, 0.1, OP.mult, OP.add)
                dmy(v)
            P.emit("V", gf_coef)
            if "gf" in dbg:
                dma_load("G", dout("gfrow", [1, 16]), gfrow[:], reads=("gfrow",))
                dma_load("G", dout("meanv", [128, 10]), gfsm[:], reads=("gfsm",))

        barrier()

        for it in (range(NTILE) if go("seed") else ()):
            for g in range(NG):
                tok0 = it * TILE + g * 128
                dma_load("Q", st_tm[:, g * D:(g + 1) * D], ln2_sp[tok0:tok0 + 128, :], reads=("ln2_sp",), writes=("st_tm",))
            for cyc in (range(3) if go("cyc") else ()):
                rw("V", ("st_tm",), ("st_tmb",))
                P.emit("V", lambda v: v.tensor_copy(st_tmb, st_tm))
                for g in range(NG):
                    for half in range(2):
                        rw("T", ("st_tmb", "ident_bs", "seed_ev"), ("bank_tb",))
                        def se_t(t, g=g, half=half):
                            for u in range(4):
                                f = half * 4 + u
                                t.transpose(bank_tb.ap()[:, u * 128:(u + 1) * 128],
                                            st_tmb[:, g * D + f * 128:g * D + (f + 1) * 128], ident_bs[:])
                        P.emit("T", se_t)
                        rw("S", ("bank_tb",), ("st_fmb", "seed_ev"))
                        def se_s(s, g=g, half=half):
                            dst = st_fmb.rearrange("p (f r) -> p f r", r=TILE)[:, half * 4:(half + 1) * 4, g * 128:(g + 1) * 128]
                            s.copy(dst, bank_tb.ap()[:, 0:512].rearrange("p (f r) -> p f r", r=128))
                        P.emit("S", se_s)
                for o in range(8):
                    bank = bank_mm if o % 2 == 0 else bank_mm2
                    bkk = "bank_mm" if o % 2 == 0 else "bank_mm2"
                    rw("T", ("st_fmb", bkk + "r"), (bkk,))
                    def c1_t(t, o=o, bank=bank):
                        for kk in range(8):
                            t.matmul(bank.ap()[:, 0:TILE], cnw1_b[kk][:, o * 128:(o + 1) * 128],
                                     st_fmb[:, kk * TILE:(kk + 1) * TILE], start=(kk == 0), stop=(kk == 7))
                    P.emit("T", c1_t)
                    rw("S", (bkk, "b_cnb1"), ("h1_b", bkk + "r"))
                    def c1_s(s, o=o, bank=bank):
                        s.activation(h1_b[:, o * TILE:(o + 1) * TILE], bank.ap()[:, 0:TILE], AF.Gelu,
                                     bias=bias_f["cnb1"].ap()[:, o:o + 1])
                    P.emit("S", c1_s)
                for o in range(4):
                    bank = bank_mm if o % 2 == 0 else bank_mm2
                    bkk = "bank_mm" if o % 2 == 0 else "bank_mm2"
                    rw("T", ("h1_b", bkk + "r"), (bkk,))
                    def c2_t(t, o=o, bank=bank):
                        for kk in range(8):
                            t.matmul(bank.ap()[:, 0:TILE], cnw2_b[kk][:, o * 128:(o + 1) * 128],
                                     h1_b[:, kk * TILE:(kk + 1) * TILE], start=(kk == 0), stop=(kk == 7))
                    P.emit("T", c2_t)
                    rw("S", (bkk, "b_cnb2"), ("h2_b", bkk + "r"))
                    def c2_s(s, o=o, bank=bank):
                        s.activation(h2_b[:, o * TILE:(o + 1) * TILE], bank.ap()[:, 0:TILE], AF.Gelu,
                                     bias=bias_f["cnb2"].ap()[:, o:o + 1])
                    P.emit("S", c2_s)
                for o in range(2):
                    bank = bank_mm if o % 2 == 0 else bank_mm2
                    bkk = "bank_mm" if o % 2 == 0 else "bank_mm2"
                    rw("T", ("h2_b", bkk + "r"), (bkk,))
                    def c3_t(t, o=o, bank=bank):
                        for kk in range(4):
                            t.matmul(bank.ap()[:, 0:TILE], cnw3_b[kk][:, o * 128:(o + 1) * 128],
                                     h2_b[:, kk * TILE:(kk + 1) * TILE], start=(kk == 0), stop=(kk == 3))
                    P.emit("T", c3_t)
                    rw("S", (bkk, "b_cnb3"), ("h3_b", bkk + "r"))
                    def c3_s(s, o=o, bank=bank):
                        s.activation(h3_b[:, o * TILE:(o + 1) * TILE], bank.ap()[:, 0:TILE], AF.Gelu,
                                     bias=bias_f["cnb3"].ap()[:, o:o + 1])
                    P.emit("S", c3_s)
                rw("T", ("h3_b", "pads", "tf_ev"), ("bank_s1",))
                def c4_t(t):
                    for kk in range(2):
                        t.matmul(bank_s1.ap()[:, 0:TILE], cnw4_pad.ap()[:, kk * 128:(kk + 1) * 128],
                                 h3_b[:, kk * TILE:(kk + 1) * TILE], start=(kk == 0), stop=(kk == 1))
                P.emit("T", c4_t)
                rw("S", ("bank_s1", "b_cnb4"), ("tf_row", "tf_ev"))
                P.emit("S", lambda s: s.activation(tf_row[0:1, :], bank_s1.ap()[0:1, 0:TILE], AF.Sigmoid,
                                                   bias=bias_f["cnb4"].ap()[:, 0:1]))
                rw("V", ("tf_row", "gfrow"), ("crow_f",))
                P.emit("V", lambda v: v.tensor_scalar(crow_f[0:1, :], tf_row[0:1, :], gfrow.ap()[:, 1:2],
                                                      gfrow.ap()[:, 2:3], OP.mult, OP.add))
                if "tf" in dbg and it == 0 and cyc == 0:
                    dma_load("G", dout("tf0", [1, TILE]), tf_row[0:1, :], reads=("tf_row",))
                    dma_load("G", dout("crow0", [1, TILE]), crow_f[0:1, :], reads=("crow_f",))
                dma_load("G", crow_sp[0:TILE].rearrange("(q d) -> q d", q=1), crow_f[0:1, :], reads=("crow_f",), writes=("crow_sp",))
                for g in range(NG):
                    dma_load("G", ccol[:, g:g + 1],
                             crow_sp[g * 128:(g + 1) * 128].rearrange("(p q) -> p q", q=1),
                             reads=("crow_sp",), writes=("ccol",))
                for g in range(NG):
                    rw("V", ("ccol", "st_tm"), ("tw", "st_tm"))
                    def tw_v(v, g=g):
                        cc = ccol[:, g:g + 1]
                        q = [st_tm[:, g * D + i * 256: g * D + (i + 1) * 256] for i in range(4)]
                        v.tensor_scalar(tw1[:, 0:256], q[2], cc, None, OP.mult)
                        v.tensor_scalar(tw1[:, 256:512], q[0], cc, None, OP.mult)
                        v.tensor_scalar(tw2[:, 0:256], q[3], cc, None, OP.mult)
                        v.tensor_scalar(tw2[:, 256:512], q[1], cc, None, OP.mult)
                        v.tensor_tensor(q[0], q[0], tw1[:, 0:256], OP.add)
                        v.tensor_tensor(q[2], q[2], tw1[:, 256:512], OP.subtract)
                        v.tensor_tensor(q[1], q[1], tw2[:, 0:256], OP.subtract)
                        v.tensor_tensor(q[3], q[3], tw2[:, 256:512], OP.add)
                    P.emit("V", tw_v)
            for g in (range(NG) if go("all") else ()):
                xc = xc_t[g % 2]; xck = f"xc_t{g % 2}"
                tok0 = it * TILE + g * 128
                dma_load("Q", xc, x1_sp[tok0:tok0 + 128, :], reads=("x1_sp",), writes=(xck,))
                rw("V", (xck, "st_tm"), ("outf_t",))
                def fi_v(v, g=g, xc=xc):
                    v.tensor_tensor(outf_t[:, 0:D], st_tm[:, g * D:(g + 1) * D], xc, OP.add)
                P.emit("V", fi_v)
                dma_load("G", out[tok0:tok0 + 128, :], outf_t[:, 0:D], reads=("outf_t",))

        P.ops["G"].append(("wait", ("G", P.cnt["G"])))
        P.ops["Q"].append(("wait", ("Q", P.cnt["Q"])))

        block = es.enter_context(nc.Block())

        def replay(elist, eng, sem_self):
            for kind, payload in elist:
                if kind == "op":
                    payload(eng)
                elif kind == "inc":
                    eng.drain().then_inc(sem_self, 1)
                elif kind == "dma":
                    payload(eng, sem_self)
                elif kind == "wait":
                    src, val = payload
                    eng.wait_ge(SEM[src], val)

        @block.tensor
        def _(t):
            replay(P.ops["T"], t, SEM["T"])

        @block.vector
        def _(v):
            replay(P.ops["V"], v, SEM["V"])

        @block.scalar
        def _(s):
            replay(P.ops["S"], s, SEM["S"])

        @block.sync
        def _(q):
            replay(P.ops["Q"], q, SEM["Q"])

        @block.gpsimd
        def _(g):
            replay(P.ops["G"], g, SEM["G"])

    return nc, sorted(d_dbg)


_CACHE = {}

def _consts():
    ident = np.eye(128, dtype=np.float32)
    import ml_dtypes
    identb = ident.astype(ml_dtypes.bfloat16)
    maskT = (np.arange(128)[:, None] % 16 == np.arange(128)[None, :] % 16).astype(np.float32)
    ones_f = np.ones((1, 128), np.float32)
    ones_b = np.ones((1, 128), ml_dtypes.bfloat16)
    ones2_b = np.zeros((2, 128), ml_dtypes.bfloat16)
    ones2_b[0] = 1
    return ident, identb, maskT, ones_f, ones_b, ones2_b


def make_inputs(core_x, W_qkv, b_qkv, W_out, b_out, qw, phases, gates, logits,
                cn, gc, ar):
    ident, identb, maskT, ones_f, ones_b, ones2_b = _consts()
    ch = np.arange(128)
    phivec = np.ascontiguousarray(phases[ch % 16, ch].reshape(128, 1))
    th = np.ascontiguousarray(gates[:, :7, 0].reshape(1, 21))
    ph = np.ascontiguousarray(gates[:, :7, 1].reshape(1, 21))
    d = {
        "x": np.ascontiguousarray(core_x),
        "wqkv_mov": np.ascontiguousarray(W_qkv.T),
        "bqkv_row": np.ascontiguousarray(b_qkv.reshape(1, -1)),
        "wout_mov": np.ascontiguousarray(W_out.T),
        "bout_row": np.ascontiguousarray(b_out.reshape(1, -1)),
        "cnw1": np.ascontiguousarray(cn[0]), "cnb1": np.ascontiguousarray(cn[1].reshape(8, 128).T),
        "cnw2": np.ascontiguousarray(cn[2]), "cnb2": np.ascontiguousarray(cn[3].reshape(4, 128).T),
        "cnw3": np.ascontiguousarray(cn[4]), "cnb3": np.ascontiguousarray(cn[5].reshape(2, 128).T),
        "cnw4": np.ascontiguousarray(cn[6]), "cnb4": np.ascontiguousarray(cn[7].reshape(1, 1)),
        "gcw1": np.ascontiguousarray(gc[0]), "gcb1": np.ascontiguousarray(gc[1].reshape(4, 128).T),
        "gcw2": np.ascontiguousarray(gc[2]), "gcb2": np.ascontiguousarray(gc[3].reshape(2, 128).T),
        "gcw3": np.ascontiguousarray(gc[4]), "gcb3": np.ascontiguousarray(gc[5].reshape(1, 1)),
        "ar": np.ascontiguousarray(ar.reshape(1, 1)),
        "phicol": phivec,
        "th_row": th, "ph_row": ph,
        "qw00_row": np.ascontiguousarray(qw[:, 0, 0].reshape(1, 4)),
        "logits_row": np.ascontiguousarray(logits.reshape(1, 4)),
        "ident_f": ident, "ident_b": identb, "maskT_f": maskT,
        "ones_row_f": ones_f, "ones_row_b": ones_b, "ones2_row_b": ones2_b,
    }
    return d


def kernel(L0_attn_query, L0_attn_W_qkv, L0_attn_b_qkv, L0_attn_W_out, L0_attn_b_out,
           L0_attn_quantum_weights, L0_attn_photonic_phases, L0_attn_entanglement_gates,
           L0_attn_superposition_logits,
           L0_ffn_cn_w1, L0_ffn_cn_b1, L0_ffn_cn_w2, L0_ffn_cn_b2, L0_ffn_cn_w3, L0_ffn_cn_b3,
           L0_ffn_cn_w4, L0_ffn_cn_b4,
           L0_ffn_gc_w1, L0_ffn_gc_b1, L0_ffn_gc_w2, L0_ffn_gc_b2, L0_ffn_gc_w3, L0_ffn_gc_b3,
           L0_ffn_adaptive_range):
    xq = np.asarray(L0_attn_query, np.float32)
    Bq, S, E = xq.shape
    assert Bq == 8 and E == D
    if "prog" not in _CACHE or _CACHE.get("S") != S:
        nc, _ = build(S=S)
        _CACHE["prog"] = nc
        _CACHE["S"] = S
    nc = _CACHE["prog"]
    cn = [np.asarray(a, np.float32) for a in (L0_ffn_cn_w1, L0_ffn_cn_b1, L0_ffn_cn_w2, L0_ffn_cn_b2,
                                              L0_ffn_cn_w3, L0_ffn_cn_b3, L0_ffn_cn_w4, L0_ffn_cn_b4)]
    gc = [np.asarray(a, np.float32) for a in (L0_ffn_gc_w1, L0_ffn_gc_b1, L0_ffn_gc_w2, L0_ffn_gc_b2,
                                              L0_ffn_gc_w3, L0_ffn_gc_b3)]
    in_maps = []
    for c in range(8):
        in_maps.append(make_inputs(
            xq[c], np.asarray(L0_attn_W_qkv, np.float32), np.asarray(L0_attn_b_qkv, np.float32),
            np.asarray(L0_attn_W_out, np.float32), np.asarray(L0_attn_b_out, np.float32),
            np.asarray(L0_attn_quantum_weights, np.float32),
            np.asarray(L0_attn_photonic_phases, np.float32),
            np.asarray(L0_attn_entanglement_gates, np.float32),
            np.asarray(L0_attn_superposition_logits, np.float32),
            cn, gc, np.asarray(L0_ffn_adaptive_range, np.float32)))
    res = run_bass_kernel_spmd(nc, in_maps, core_ids=list(range(8)))
    return np.stack([np.asarray(r["out"], np.float32) for r in res.results], axis=0)
